# version 09
# baseline (speedup 1.0000x reference)
import numpy as np
import ml_dtypes
import concourse.bass as bass
import concourse.mybir as mybir
from concourse.bass_utils import run_bass_kernel_spmd

F32 = mybir.dt.float32
BF16 = mybir.dt.bfloat16
AF = mybir.ActivationFunctionType
ALU = mybir.AluOpType

ENGINES = ("PE", "ACT", "DVE", "SP", "POOL")


class Cfg:
    def __init__(self, D=2048, DH=512, INTER=8192, T=1024, n_cores=8, PANEL_NC=16):
        self.D = D
        self.R = D
        self.DH = DH
        self.INTER = INTER
        self.T = T
        self.TH = T // 2
        self.TBW = T // 2
        self.n_cores = n_cores
        self.NCD = D // 128
        self.NCR = self.R // 128
        self.NCH = DH // 128
        self.NCI = INTER // 128
        self.PANEL_NC = PANEL_NC
        self.NPANEL = self.NCI // PANEL_NC
        self.PANEL = PANEL_NC * 128
        assert self.TH <= 512 and self.TBW <= 512
        assert self.NPANEL * PANEL_NC == self.NCI
        self.MAXW = max(D, DH, self.PANEL)



def tile_w(A):
    K, M = A.shape
    KC, MC = K // 128, M // 128
    return np.ascontiguousarray(
        A.reshape(KC, 128, MC, 128).transpose(2, 1, 0, 3).reshape(MC, 128, KC * 128)
    )


def feat_major_vec(b):
    F = b.shape[0]
    return np.ascontiguousarray(b.reshape(F // 128, 128).T)



class Builder:
    def __init__(self, sem_names):
        self.ops = {e: [] for e in ENGINES}
        self.cnt = {n: 0 for n in sem_names}
        self.waited = {e: {} for e in ENGINES}
        self.seq = False
        self.strip_i = 0
        self.cast_val = {}
        self.pe_val_strip = {}

    def emit(self, eng, fn, deps=(), inc=None, inc_amt=1):
        if self.seq:
            deps = list(deps) + [(s, v) for s, v in self.cnt.items() if v > 0]
        real = []
        w = self.waited[eng]
        for s, v in deps:
            if v > 0 and w.get(s, 0) < v:
                real.append((s, v))
                w[s] = v
        self.ops[eng].append((tuple(real), fn, inc, inc_amt))
        if inc is not None:
            self.cnt[inc] += inc_amt
            return self.cnt[inc]
        return None

    def wait(self, eng, deps):
        self.emit(eng, None, deps=deps)


class Bank:
    __slots__ = ("t", "last", "mmv")

    def __init__(self, t):
        self.t = t
        self.last = None



def build_program(cfg, stage=None, dbg_nog=False, dbg_noG_read=False, seq=False):
    c = cfg
    nc = bass.Bass("TRN2", target_bir_lowering=False)

    dt = {}
    def din(name, shape):
        dt[name] = nc.dram_tensor(name, list(shape), F32, kind="ExternalInput")
        return dt[name]

    din("x_t", (c.D, c.T))
    din("xe", (128, 4 * c.NCD))
    din("wg1", (c.NCH, 128, c.D))
    din("wg2", (c.NCR, 128, c.DH))
    din("wu", (c.NCR, 128, c.D))
    din("wv", (c.NCR, 128, c.D))
    din("wgug", (c.NCI, 128, c.D))
    din("wguu", (c.NCI, 128, c.D))
    din("wd", (c.NCD, 128, c.INTER))
    din("bu_t", (128, c.NCR))
    din("bv_t", (128, c.NCR))
    din("bg1_t", (128, c.NCH))
    din("bg2_t", (128, c.NCR))
    din("cb_t", (128, c.NCD))
    din("cw_t", (128, 3 * c.NCD))
    din("mask", (8, 1))
    out_d = nc.dram_tensor("out", [c.D, c.T], F32, kind="ExternalOutput")
    cc_in = nc.dram_tensor("cc_in", [128, c.NCR], F32)
    cc_out = nc.dram_tensor("cc_out", [c.n_cores * 128, c.NCR], F32)

    sb = {}
    def s(name, shape, dtype):
        sb[name] = nc.alloc_sbuf_tensor(name, list(shape), dtype)
        return sb[name]

    A = s("A", (128, c.NCD * c.T), F32)
    X = s("X", (128, c.NCD * c.TH), BF16)
    G = s("G", (128, c.NCR * c.TH), BF16)
    U = s("U", (128, c.NCR * c.T), BF16)
    V = s("V", (128, c.NCR * c.T), BF16)
    Hh = s("Hh", (128, c.NCH * c.TH), BF16)
    w32 = [s(f"w32_{i}", (128, c.MAXW), F32) for i in range(2)]
    w16 = [s(f"w16_{i}", (128, c.MAXW), BF16) for i in range(2)]
    scr_n = max(2 * (c.TH + 2) + 2 * c.TH, 2 * c.TBW, c.NCR * 128)
    SCR = s("SCR", (128, scr_n), F32)
    zeros = s("zeros", (128, c.TH), BF16)
    rows = s("rows", (1, 3 * c.TBW), F32)
    S_sb = s("S_sb", (128, c.NCR), F32)
    C_sb = s("C_sb", (128, c.NCR), F32)
    bu_s = s("bu_s", (128, c.NCR), F32)
    bv_s = s("bv_s", (128, c.NCR), F32)
    bg1_s = s("bg1_s", (128, c.NCH), F32)
    bg2_s = s("bg2_s", (128, c.NCR), F32)
    cb_s = s("cb_s", (128, c.NCD), F32)
    cw_s = s("cw_s", (128, 3 * c.NCD), F32)
    xe_s = s("xe_s", (128, 4 * c.NCD), F32)
    mask_s = s("mask_s", (8, 1), F32)
    ones_b = s("ones_b", (128, 1), BF16)
    ones_f = s("ones_f", (1, 128), F32)
    eps_s = s("eps_s", (1, 1), F32)

    TH2 = c.TH + 2
    tmp0 = SCR.ap()[:, 2 * TH2: 2 * TH2 + c.TH]
    tmp1 = SCR.ap()[:, 2 * TH2 + c.TH: 2 * TH2 + 2 * c.TH]
    S_all = SCR.ap()[0:c.n_cores, 0:c.NCR * 128]
    sg = [SCR.ap()[:, i * c.TBW:(i + 1) * c.TBW] for i in range(2)]

    pm = [Bank(nc.alloc_psum_tensor(f"pm{i}", [128, 512], F32)) for i in range(4)]
    pbc = [Bank(nc.alloc_psum_tensor(f"pbc{i}", [128, 512], F32)) for i in range(2)]
    pst = [Bank(nc.alloc_psum_tensor(f"pst{i}", [128, 512], F32)) for i in range(2)]
    bank_rot = [0]

    def next_bank():
        b = pm[bank_rot[0] % 4]
        bank_rot[0] += 1
        return b

    sem_names = ["pe", "act", "dve", "pool", "cc",
                 "wdma0", "wdma1", "xdma0", "xdma1", "pdma", "odma", "cdma"]
    semh = {n: nc.alloc_semaphore(n) for n in sem_names}
    B = Builder(sem_names)
    B.seq = seq
    B.scr_last = {0: None, 1: None}

    for name, tgt in (("bu_t", bu_s), ("bv_t", bv_s), ("bg1_t", bg1_s),
                      ("bg2_t", bg2_s), ("cb_t", cb_s), ("cw_t", cw_s),
                      ("xe", xe_s), ("mask", mask_s)):
        B.emit("SP", lambda e, i=dt[name].ap(), o=tgt.ap(): e.dma_start(out=o, in_=i),
               inc="pdma", inc_amt=16)
    bias_val = B.cnt["pdma"]
    B.emit("DVE", lambda e: e.memset(zeros.ap(), 0.0), inc="dve")
    B.emit("DVE", lambda e: e.memset(ones_b.ap(), 1.0), inc="dve")
    B.emit("DVE", lambda e: e.memset(ones_f.ap(), 1.0), inc="dve")
    B.emit("DVE", lambda e: e.memset(eps_s.ap(), 1e-5), inc="dve")
    const_val = B.cnt["dve"]

    def load_strip(dram_ap, width, extra_cast_deps=()):
        si = B.strip_i
        B.strip_i += 1
        slot = si % 2
        dma_sem = "wdma0" if si % 2 == 0 else "wdma1"
        deps = []
        if si - 2 in B.cast_val:
            deps.append(("act", B.cast_val[si - 2]))
        B.emit("SP", lambda e, i=dram_ap, o=w32[slot].ap()[:, 0:width]:
               e.dma_start(out=o, in_=i), deps=deps, inc=dma_sem, inc_amt=16)
        dv = B.cnt[dma_sem]
        cdeps = [(dma_sem, dv)] + list(extra_cast_deps)
        if si - 2 in B.pe_val_strip:
            cdeps.append(("pe", B.pe_val_strip[si - 2]))
        cv = B.emit("ACT", lambda e, o=w16[slot].ap()[:, 0:width],
                    i=w32[slot].ap()[:, 0:width]: e.copy(o, i),
                    deps=cdeps, inc="act")
        B.cast_val[si] = cv
        return si, slot, cv

    def mm_group(si, slot, KC, rhs_fn, W, extra_deps=(), drain=True,
                 kc_dep=None):
        bank = next_bank()
        deps = [("act", B.cast_val[si])] + list(extra_deps)
        if bank.last:
            deps.append(bank.last)
        for kc in range(KC):
            first, last = kc == 0, kc == KC - 1
            d = list(deps) if first else []
            if kc_dep is not None:
                d.append(kc_dep(kc))
            def mk(e, kc=kc, bank=bank, first=first, last=last):
                return e.matmul(bank.t.ap()[:, 0:W],
                                w16[slot].ap()[:, kc * 128:(kc + 1) * 128],
                                rhs_fn(kc),
                                start=first, stop=last)
            B.emit("PE", mk, deps=d)
        if not drain:
            return bank, None
        mmv = B.emit("PE", lambda e: e.drain(), inc="pe")
        B.pe_val_strip[si] = mmv
        return bank, mmv

    p0_done = {}

    def phase_p0_load(h, queue):
        dsem = "xdma0" if h == 0 else "xdma1"
        for cd in range(c.NCD):
            src = dt["x_t"].ap()[cd * 128:(cd + 1) * 128,
                                 h * c.TH:(h + 1) * c.TH]
            asl = A.ap()[:, cd * c.T + h * c.TH: cd * c.T + (h + 1) * c.TH]
            B.emit(queue, lambda e, i=src, o=asl: e.dma_start(out=o, in_=i),
                   inc=dsem, inc_amt=16)

    def phase_p0_cast(h, pe_guard, extra=()):
        dsem = "xdma0" if h == 0 else "xdma1"
        for cd in range(c.NCD):
            asl = A.ap()[:, cd * c.T + h * c.TH: cd * c.T + (h + 1) * c.TH]
            xsl = X.ap()[:, cd * c.TH:(cd + 1) * c.TH]
            adeps = [(dsem, 16 * c.NCD)] + list(extra)
            if pe_guard is not None:
                adeps.append(pe_guard)
            if cd % 2 == 0:
                B.emit("DVE", lambda e, asl=asl, xsl=xsl: e.tensor_copy(
                    xsl, asl), deps=adeps, inc="dve")
            else:
                B.emit("ACT", lambda e, asl=asl, xsl=xsl: e.copy(xsl, asl),
                       deps=adeps, inc="act")
        p0_done[h] = (("act", B.cnt["act"]), ("dve", B.cnt["dve"]))

    def phase_conv(h):
        cast_dep = p0_done[h][0]
        cast_dep_dve = p0_done[h][1]
        for cd in range(c.NCD):
            w0 = cw_s.ap()[:, 0 * c.NCD + cd: 0 * c.NCD + cd + 1]
            w1 = cw_s.ap()[:, 1 * c.NCD + cd: 1 * c.NCD + cd + 1]
            w2 = cw_s.ap()[:, 2 * c.NCD + cd: 2 * c.NCD + cd + 1]
            cb = cb_s.ap()[:, cd:cd + 1]
            asl = A.ap()[:, cd * c.T + h * c.TH: cd * c.T + (h + 1) * c.TH]
            xsl = X.ap()[:, cd * c.TH:(cd + 1) * c.TH]
            d0 = [cast_dep, cast_dep_dve, ("pdma", bias_val),
                  ("dve", B.cnt["dve"])]
            B.emit("DVE", lambda e, asl=asl, xsl=xsl, w1=w1:
                   e.scalar_tensor_tensor(asl, xsl, w1, asl,
                                          ALU.mult, ALU.add),
                   deps=d0, inc="dve")
            B.emit("DVE", lambda e, asl=asl, cb=cb:
                   e.tensor_scalar_add(asl, asl, cb),
                   deps=[("dve", B.cnt["dve"])], inc="dve")
            B.emit("DVE", lambda e, asl=asl, xsl=xsl, w0=w0:
                   e.scalar_tensor_tensor(asl[:, 1:c.TH], xsl[:, 0:c.TH - 1],
                                          w0, asl[:, 1:c.TH],
                                          ALU.mult, ALU.add),
                   deps=[("dve", B.cnt["dve"])], inc="dve")
            B.emit("DVE", lambda e, asl=asl, xsl=xsl, w2=w2:
                   e.scalar_tensor_tensor(asl[:, 0:c.TH - 1], xsl[:, 1:c.TH],
                                          w2, asl[:, 0:c.TH - 1],
                                          ALU.mult, ALU.add),
                   deps=[("dve", B.cnt["dve"])], inc="dve")
            eL = xe_s.ap()[:, 4 * cd + 2 * h: 4 * cd + 2 * h + 1]
            eR = xe_s.ap()[:, 4 * cd + 2 * h + 1: 4 * cd + 2 * h + 2]
            B.emit("DVE", lambda e, asl=asl, eL=eL, w0=w0:
                   e.scalar_tensor_tensor(asl[:, 0:1], eL, w0, asl[:, 0:1],
                                          ALU.mult, ALU.add),
                   deps=[("dve", B.cnt["dve"])], inc="dve")
            B.emit("DVE", lambda e, asl=asl, eR=eR, w2=w2:
                   e.scalar_tensor_tensor(asl[:, c.TH - 1:c.TH], eR, w2,
                                          asl[:, c.TH - 1:c.TH],
                                          ALU.mult, ALU.add),
                   deps=[("dve", B.cnt["dve"])], inc="dve")

    def x_rhs(kc):
        return X.ap()[:, kc * c.TH:(kc + 1) * c.TH]

    def phase_h(h):
        xdep = [p0_done[h][0], p0_done[h][1]]
        pend = load_strip(dt["wg1"].ap()[0], c.D)
        for oc in range(c.NCH):
            si, slot, _ = pend
            bank, mmv = mm_group(si, slot, c.NCD, x_rhs, c.TH,
                                 extra_deps=xdep if oc == 0 else ())
            if oc + 1 < c.NCH:
                pend = load_strip(dt["wg1"].ap()[oc + 1], c.D)
            osl = Hh.ap()[:, oc * c.TH:(oc + 1) * c.TH]
            ev = B.emit("ACT", lambda e, b=bank, osl=osl, oc=oc: e.activation(
                osl, b.t.ap()[:, 0:c.TH], AF.Relu,
                bias=bg1_s.ap()[:, oc:oc + 1], scale=1.0),
                deps=[("pe", mmv), ("pdma", bias_val)], inc="act")
            bank.last = ("act", ev)

    def phase_gates(h, gdep):
        hdep = [("act", B.cnt["act"])] + gdep
        pend = load_strip(dt["wg2"].ap()[0], c.DH)
        for oc in range(c.NCR):
            si, slot, _ = pend
            bank, mmv = mm_group(si, slot, c.NCH,
                                 lambda kc: Hh.ap()[:, kc * c.TH:(kc + 1) * c.TH],
                                 c.TH, extra_deps=hdep if oc == 0 else ())
            if oc + 1 < c.NCR:
                pend = load_strip(dt["wg2"].ap()[oc + 1], c.DH)
            osl = G.ap()[:, oc * c.TH:(oc + 1) * c.TH]
            deps = [("pe", mmv)] + (gdep if oc == 0 else [])
            ev = B.emit("ACT", lambda e, b=bank, osl=osl, oc=oc: e.activation(
                osl, b.t.ap()[:, 0:c.TH], AF.Sigmoid,
                bias=bg2_s.ap()[:, oc:oc + 1], scale=1.0),
                deps=deps, inc="act")
            bank.last = ("act", ev)

    def phase_uv_proj(h, which):
        wname, bias_t, dst = (("wu", bu_s, U) if which == "u" else ("wv", bv_s, V))
        pend = load_strip(dt[wname].ap()[0], c.D)
        for oc in range(c.NCR):
            si, slot, _ = pend
            bank, mmv = mm_group(si, slot, c.NCD, x_rhs, c.TH)
            if oc + 1 < c.NCR:
                pend = load_strip(dt[wname].ap()[oc + 1], c.D)
            osl = dst.ap()[:, oc * c.T + h * c.TH: oc * c.T + (h + 1) * c.TH]
            gsl = G.ap()[:, oc * c.TH:(oc + 1) * c.TH]
            av = B.emit("ACT", lambda e, b=bank, osl=osl, oc=oc,
                        bias_t=bias_t: e.activation(
                osl, b.t.ap()[:, 0:c.TH], AF.Identity,
                bias=bias_t.ap()[:, oc:oc + 1], scale=1.0),
                deps=[("pe", mmv), ("pdma", bias_val)], inc="act")
            bank.last = ("act", av)
            if not dbg_nog:
                B.emit("DVE", lambda e, osl=osl, gsl=gsl: e.tensor_tensor(
                    osl, osl, gsl, ALU.mult),
                    deps=[("act", av), ("dve", B.cnt["dve"])], inc="dve")
                if which == "u":
                    init = (0.0 if h == 0 else
                            U.ap()[:, oc * c.T + c.TH - 1: oc * c.T + c.TH])
                    B.emit("DVE", lambda e, osl=osl, init=init:
                           e.tensor_tensor_scan(osl, osl, zeros.ap()[:, 0:c.TH],
                                                init, ALU.add, ALU.add),
                           deps=[("dve", B.cnt["dve"])], inc="dve")

    def phase_scan(h):
        udep = ("act", B.cnt["act"])
        for cd in range(c.NCR):
            usl = U.ap()[:, cd * c.T + h * c.TH: cd * c.T + (h + 1) * c.TH]
            if h == 0:
                init = 0.0
            else:
                init = U.ap()[:, cd * c.T + c.TH - 1: cd * c.T + c.TH]
            B.emit("DVE", lambda e, usl=usl, init=init: e.tensor_tensor_scan(
                usl, usl, zeros.ap()[:, 0:c.TH], init, ALU.add, ALU.add),
                deps=[("dve", B.cnt["dve"]), udep], inc="dve")

    def phase_carry_fire():
        sv = B.emit("DVE", lambda e: e.tensor_copy(
            S_sb.ap()[:, 0:c.NCR], U.ap()[:, c.T - 1::c.T]),
            deps=[("dve", B.cnt["dve"])], inc="dve")
        B.emit("POOL", lambda e: e.dma_start(out=cc_in.ap(), in_=S_sb.ap()),
               deps=[("dve", sv)], inc="cdma", inc_amt=16)
        B.wait("POOL", [("cdma", B.cnt["cdma"])])
        B.emit("POOL", lambda e: e.collective_compute(
            "AllGather", ALU.bypass,
            replica_groups=[list(range(c.n_cores))],
            ins=[cc_in.ap()], outs=[cc_out.ap()]), inc="cc")
        B.wait("POOL", [("cc", B.cnt["cc"])])
        sall_dep = [("cc", B.cnt["cc"]),
                    ("dve", B.cnt["dve"]), ("act", B.cnt["act"])]
        B.emit("POOL", lambda e: e.dma_start(
            out=S_all, in_=cc_out.ap().rearrange("(n f) c -> n (f c)", n=c.n_cores)),
            deps=sall_dep, inc="cdma", inc_amt=16)
        return B.cnt["cdma"]

    def phase_carry_finish(pv):
        bank = pbc[0]
        deps = [("cdma", pv), ("pdma", bias_val)]
        if bank.last:
            deps.append(bank.last)
        for cd in range(c.NCR):
            def mk(e, cd=cd):
                return e.matmul(bank.t.ap()[:, cd:cd + 1],
                                S_all[:, cd::c.NCR],
                                mask_s.ap()[0:c.n_cores, 0:1],
                                start=True, stop=True)
            B.emit("PE", mk, deps=(deps if cd == 0 else ()))
        mmv = B.emit("PE", lambda e: e.drain(), inc="pe")
        cv = B.emit("ACT", lambda e: e.copy(
            C_sb.ap()[:, 0:c.NCR], bank.t.ap()[:, 0:c.NCR]),
            deps=[("pe", mmv)], inc="act")
        bank.last = ("act", cv)
        return ("act", cv)

    def phase_go(cdep):
        go_val = {}
        for cd in range(c.NCR):
            for h in range(2):
                ksl = U.ap()[:, cd * c.T + h * c.TH: cd * c.T + (h + 1) * c.TH]
                vsl = V.ap()[:, cd * c.T + h * c.TH: cd * c.T + (h + 1) * c.TH]
                v = B.emit("DVE", lambda e, ksl=ksl, vsl=vsl, cd=cd:
                           e.scalar_tensor_tensor(vsl, ksl,
                                                  C_sb.ap()[:, cd:cd + 1],
                                                  vsl, ALU.add, ALU.mult),
                           deps=[("dve", B.cnt["dve"]), cdep], inc="dve")
                go_val[(cd, h)] = v
        return go_val

    def phase_uvmm(go_val):
        pend = load_strip(dt["wu"].ap()[0], c.R)
        gi = [0]
        for oc in range(c.NCD):
            si, slot, _ = pend
            grps = []
            for tb in range(2):
                bank, mmv = mm_group(
                    si, slot, c.NCR,
                    lambda kc, tb=tb: V.ap()[:, kc * c.T + tb * c.TBW:
                                             kc * c.T + (tb + 1) * c.TBW],
                    c.TBW,
                    drain=(tb == 1),
                    kc_dep=((lambda kc, tb=tb: ("dve", go_val[(kc, tb)]))
                            if oc == 0 else None))
                grps.append((tb, bank, mmv))
            mmv = grps[1][2]
            grps[0] = (0, grps[0][1], mmv)
            B.pe_val_strip[si] = B.cnt["pe"]
            if oc + 1 < c.NCD:
                pend = load_strip(dt["wu"].ap()[oc + 1], c.R)
            for tb, bank, mmv in grps:
                sl = gi[0] % 2
                gi[0] += 1
                sdeps = [("pe", mmv)]
                if B.scr_last[sl]:
                    sdeps.append(B.scr_last[sl])
                av = B.emit("ACT", lambda e, b=bank, sl=sl: e.copy(
                    sg[sl], b.t.ap()[:, 0:c.TBW]), deps=sdeps, inc="act")
                bank.last = ("act", av)
                asl = A.ap()[:, oc * c.T + tb * c.TBW: oc * c.T + (tb + 1) * c.TBW]
                dv = B.emit("DVE", lambda e, asl=asl, sl=sl: e.tensor_tensor(
                    asl, asl, sg[sl], ALU.add),
                    deps=[("act", av), ("dve", B.cnt["dve"])], inc="dve")
                B.scr_last[sl] = ("dve", dv)

    def phase_ln(do_x2b=False):
        a3 = A.ap().rearrange("p (c t) -> p c t", t=c.T)
        x3 = X.ap().rearrange("p (c t) -> p c t", t=c.TH)
        g3 = G.ap().rearrange("p (c t) -> p c t", t=c.TH)
        u3 = U.ap().rearrange("p (c t) -> p c t", t=c.T)
        ln_vals = []
        apply_cd = {}
        x2b_vals = []
        stats_done = None
        for tb in range(2):
            adep = [("dve", B.cnt["dve"]), ("pe", B.cnt["pe"])]
            if tb == 0:
                cast_v = B.emit("DVE", lambda e, tb=tb: e.tensor_copy(
                    x3, a3[:, :, tb * c.TBW:(tb + 1) * c.TBW]),
                    deps=adep, inc="dve")
                sq_v = B.emit("DVE", lambda e: e.tensor_tensor(
                    g3, x3, x3, ALU.mult),
                    deps=[("dve", cast_v)], inc="dve")
                sem = "dve"
            else:
                cast_v = B.emit("ACT", lambda e, tb=tb: e.copy(
                    x3, a3[:, :, tb * c.TBW:(tb + 1) * c.TBW]),
                    deps=adep + [stats_done], inc="act")
                sq_v = B.emit("ACT", lambda e, tb=tb: e.activation(
                    g3, a3[:, :, tb * c.TBW:(tb + 1) * c.TBW], AF.Square),
                    deps=[("act", cast_v)], inc="act")
                sem = "act"
            for bi, src in ((0, X), (1, G)):
                bank = pst[bi]
                deps = [(sem, cast_v if bi == 0 else sq_v)]
                if bank.last:
                    deps.append(bank.last)
                for cd in range(c.NCD):
                    def mk(e, cd=cd, bank=bank, src=src,
                           first=(cd == 0), last=(cd == c.NCD - 1)):
                        return e.matmul(bank.t.ap()[0:1, 0:c.TBW],
                                        ones_b.ap()[:, 0:1],
                                        src.ap()[:, cd * c.TH:(cd + 1) * c.TH],
                                        start=first, stop=last)
                    B.emit("PE", mk, deps=(deps if cd == 0 else ()))
                bank.mmv = B.emit("PE", lambda e: e.drain(), inc="pe")
            stats_done = ("pe", pst[1].mmv)
            r_negmu = rows.ap()[0:1, 0:c.TBW]
            r_var = rows.ap()[0:1, c.TBW:2 * c.TBW]
            r_tmp = rows.ap()[0:1, 2 * c.TBW:3 * c.TBW]
            v1 = B.emit("ACT", lambda e: e.activation(
                r_negmu, pst[0].t.ap()[0:1, 0:c.TBW], AF.Copy,
                bias=0.0, scale=-1.0 / c.D),
                deps=[("pe", pst[0].mmv)], inc="act")
            pst[0].last = ("act", v1)
            v1b = B.emit("ACT", lambda e: e.activation(
                r_var, pst[1].t.ap()[0:1, 0:c.TBW], AF.Copy,
                bias=0.0, scale=1.0 / c.D),
                deps=[("pe", pst[1].mmv), ("dve", B.cnt["dve"])], inc="act")
            pst[1].last = ("act", v1b)
            v2 = B.emit("DVE", lambda e: e.tensor_tensor(
                r_tmp, r_negmu, r_negmu, ALU.mult),
                deps=[("act", v1), ("dve", B.cnt["dve"])], inc="dve")
            v3 = B.emit("DVE", lambda e: e.tensor_tensor(
                r_var, r_var, r_tmp, ALU.subtract),
                deps=[("act", v1b), ("dve", v2)], inc="dve")
            v4 = B.emit("ACT", lambda e: e.activation(
                r_tmp, r_var, AF.Sqrt, bias=eps_s.ap()[0:1, 0:1], scale=1.0),
                deps=[("dve", v3), ("dve", const_val)], inc="act")
            v5 = B.emit("DVE", lambda e: e.reciprocal(r_var, r_tmp),
                        deps=[("act", v4), ("dve", v3)], inc="dve")
            bdeps0 = [("act", v1)] + ([pbc[0].last] if pbc[0].last else [])
            B.emit("PE", lambda e: e.matmul(
                pbc[0].t.ap()[:, 0:c.TBW], ones_f.ap()[0:1, :], r_negmu,
                start=True, stop=True), deps=bdeps0)
            bm0 = B.emit("PE", lambda e: e.drain(), inc="pe")
            bdeps1 = [("dve", v5)] + ([pbc[1].last] if pbc[1].last else [])
            B.emit("PE", lambda e: e.matmul(
                pbc[1].t.ap()[:, 0:c.TBW], ones_f.ap()[0:1, :], r_var,
                start=True, stop=True), deps=bdeps1)
            bm1 = B.emit("PE", lambda e: e.drain(), inc="pe")
            bc0v = B.emit("ACT", lambda e: e.copy(tmp0, pbc[0].t.ap()[:, 0:c.TBW]),
                          deps=[("pe", bm0), ("dve", B.cnt["dve"])], inc="act")
            bc1v = B.emit("ACT", lambda e: e.copy(tmp1, pbc[1].t.ap()[:, 0:c.TBW]),
                          deps=[("pe", bm1)], inc="act")
            ev = None
            for cd in range(c.NCD):
                asl = A.ap()[:, cd * c.T + tb * c.TBW: cd * c.T + (tb + 1) * c.TBW]
                B.emit("DVE", lambda e, asl=asl: e.tensor_tensor(
                    asl, asl, tmp0, ALU.add),
                    deps=[("act", bc0v)], inc="dve")
                ev = B.emit("DVE", lambda e, asl=asl: e.tensor_tensor(
                    asl, asl, tmp1, ALU.mult),
                    deps=[("act", bc1v), ("dve", B.cnt["dve"])], inc="dve")
                if tb == 1:
                    apply_cd[cd] = ev
            pbc[0].last = ("act", bc0v)
            pbc[1].last = ("act", bc1v)
            B.scr_tmp_last = ("dve", ev)
            ln_vals.append(ev)
            if do_x2b:
                v = B.emit("ACT", lambda e, tb=tb: e.copy(
                    u3[:, :, tb * c.TBW:(tb + 1) * c.TBW],
                    a3[:, :, tb * c.TBW:(tb + 1) * c.TBW]),
                    deps=[("dve", ev)], inc="act")
                x2b_vals.append([("act", v)])
        return ln_vals, apply_cd, x2b_vals

    def phase_ffn(x2b_done):
        descs = []
        for p in range(c.NPANEL):
            for j in range(c.PANEL_NC):
                ocg = p * c.PANEL_NC + j
                descs.append(("g", p, j, dt["wgug"].ap()[ocg], c.D))
                descs.append(("u", p, j, dt["wguu"].ap()[ocg], c.D))
            for oc in range(c.NCD):
                descs.append(("d", p, oc,
                              dt["wd"].ap()[oc][:, p * c.PANEL:(p + 1) * c.PANEL],
                              c.PANEL))
        pend = load_strip(descs[0][3], descs[0][4])
        f_done = {}
        for i, (kind, p, idx, ap_, w) in enumerate(descs):
            si, slot, _ = pend
            grps = []
            for tb in range(2):
                if kind == "d":
                    rhs = lambda kc, tb=tb: V.ap()[:, kc * c.T + tb * c.TBW:
                                                   kc * c.T + (tb + 1) * c.TBW]
                    extra = [f_done[p]] if (idx == 0 and tb == 0) else ()
                    bank, mmv = mm_group(si, slot, c.PANEL_NC, rhs, c.TBW,
                                         extra_deps=extra, drain=(tb == 1))
                else:
                    rhs = lambda kc, tb=tb: U.ap()[:, kc * c.T + tb * c.TBW:
                                                   kc * c.T + (tb + 1) * c.TBW]
                    extra = x2b_done[tb] if i == 0 else ()
                    bank, mmv = mm_group(si, slot, c.NCD, rhs, c.TBW,
                                         extra_deps=extra, drain=(tb == 1))
                grps.append((tb, bank, mmv))
            mmv = grps[1][2]
            grps[0] = (0, grps[0][1], mmv)
            B.pe_val_strip[si] = B.cnt["pe"]
            if i + 1 < len(descs):
                pend = load_strip(descs[i + 1][3], descs[i + 1][4])
            for tb, bank, mmv in grps:
                if kind == "g":
                    sdeps = [("pe", mmv)]
                    if B.scr_last[tb]:
                        sdeps.append(B.scr_last[tb])
                    sv = B.emit("ACT", lambda e, b=bank, tb=tb: e.activation(
                        sg[tb], b.t.ap()[:, 0:c.TBW], AF.Silu),
                        deps=sdeps, inc="act")
                    bank.last = ("act", sv)
                elif kind == "u":
                    fsl = V.ap()[:, idx * c.T + tb * c.TBW:
                                 idx * c.T + (tb + 1) * c.TBW]
                    av = B.emit("ACT", lambda e, b=bank, fsl=fsl: e.copy(
                        fsl, b.t.ap()[:, 0:c.TBW]), deps=[("pe", mmv)], inc="act")
                    bank.last = ("act", av)
                    fv = B.emit("DVE", lambda e, fsl=fsl, tb=tb: e.tensor_tensor(
                        fsl, fsl, sg[tb], ALU.mult),
                        deps=[("act", av), ("dve", B.cnt["dve"])], inc="dve")
                    B.scr_last[tb] = ("dve", fv)
                    if idx == c.PANEL_NC - 1 and tb == 1:
                        f_done[p] = ("dve", fv)
                else:
                    sdeps = [("pe", mmv)]
                    if B.scr_last[tb]:
                        sdeps.append(B.scr_last[tb])
                    av = B.emit("ACT", lambda e, b=bank, tb=tb: e.copy(
                        sg[tb], b.t.ap()[:, 0:c.TBW]), deps=sdeps, inc="act")
                    bank.last = ("act", av)
                    asl = A.ap()[:, idx * c.T + tb * c.TBW:
                                 idx * c.T + (tb + 1) * c.TBW]
                    dv = B.emit("DVE", lambda e, asl=asl, tb=tb: e.tensor_tensor(
                        asl, asl, sg[tb], ALU.add),
                        deps=[("act", av), ("dve", B.cnt["dve"])], inc="dve")
                    B.scr_last[tb] = ("dve", dv)

    def phase_out(apply_cd=None):
        dep = [("dve", B.cnt["dve"])]
        for cd in range(c.NCD):
            d = ([("dve", apply_cd[cd])] if apply_cd is not None
                 else (dep if cd == 0 else ()))
            B.emit("SP", lambda e, cd=cd: e.dma_start(
                out=out_d.ap()[cd * 128:(cd + 1) * 128, :],
                in_=A.ap()[:, cd * c.T:(cd + 1) * c.T]),
                deps=d, inc="odma", inc_amt=16)
        B.wait("SP", [("odma", B.cnt["odma"])])

    def dump(src_t):
        for cd in range(c.NCR):
            B.emit("DVE", lambda e, cd=cd: e.tensor_copy(
                A.ap()[:, cd * c.T:(cd + 1) * c.T],
                src_t.ap()[:, cd * c.T:(cd + 1) * c.T]),
                deps=[("dve", B.cnt["dve"]), ("act", B.cnt["act"])], inc="dve")

    def sched():
        phase_p0_load(0, "ACT")
        phase_p0_load(1, "POOL")
        phase_p0_cast(0, None)
        phase_conv(0)
        conv0_dve = ("dve", B.cnt["dve"])
        if stage == "acc0h0":
            return
        phase_h(0)
        phase_gates(0, [("dve", const_val)])
        phase_uv_proj(0, "u")
        if dbg_nog:
            phase_scan(0)
        phase_uv_proj(0, "v")
        v_h0_pe = ("pe", B.cnt["pe"])
        v_h0_dve = ("dve", B.cnt["dve"])
        phase_p0_cast(1, v_h0_pe, extra=[conv0_dve])
        phase_conv(1)
        if stage == "acc0":
            return
        phase_h(1)
        phase_gates(1, [v_h0_dve])
        phase_uv_proj(1, "u")
        if stage == "ug":
            dump(U)
            return
        if dbg_nog:
            phase_scan(1)
        if stage == "k":
            dump(U)
            return
        pv = phase_carry_fire()
        phase_uv_proj(1, "v")
        if stage == "vg":
            dump(V)
            return
        cdep = phase_carry_finish(pv)
        go_val = phase_go(cdep)
        if stage == "go":
            dump(V)
            return
        phase_uvmm(go_val)
        if stage == "res1":
            return
        _, _, x2b_done = phase_ln(do_x2b=True)
        if stage == "x2":
            return
        phase_ffn(x2b_done)
        if stage == "res2":
            return None
        _, apply_cd, _ = phase_ln()
        return apply_cd

    apply_cd = sched()
    phase_out(apply_cd)

    with nc.Block() as blk:
        def runner(ops):
            def run(eng):
                for deps, fn, inc, amt in ops:
                    for sn, v in deps:
                        eng.wait_ge(semh[sn], v)
                    if fn is not None:
                        inst = fn(eng)
                        if inc is not None:
                            inst.then_inc(semh[inc], amt)
            return run
        blk.tensor(runner(B.ops["PE"]))
        blk.scalar(runner(B.ops["ACT"]))
        blk.vector(runner(B.ops["DVE"]))
        blk.sync(runner(B.ops["SP"]))
        blk.gpsimd(runner(B.ops["POOL"]))

    return nc



def make_in_maps(cfg, x, Wu, bu, Wv, bv, Wg1, bg1, Wg2, bg2, conv_w, conv_b,
                 w_gate_up, w_down):
    c = cfg
    f32 = np.float32
    x_t = np.ascontiguousarray(x.reshape(-1, c.D).T.astype(f32))
    L = x_t.shape[1]
    x_pad = np.concatenate([np.zeros((c.D, 1), f32), x_t,
                            np.zeros((c.D, 1), f32)], axis=1)
    wg1 = tile_w(np.ascontiguousarray(Wg1.T.astype(f32)))
    wg2 = tile_w(np.ascontiguousarray(Wg2.T.astype(f32)))
    wu = tile_w(np.ascontiguousarray(Wu.T.astype(f32)))
    wv = tile_w(np.ascontiguousarray(Wv.T.astype(f32)))
    WguT = np.ascontiguousarray(w_gate_up.T.astype(f32))
    wgug = tile_w(WguT[:, :c.INTER])
    wguu = tile_w(WguT[:, c.INTER:])
    wd = tile_w(np.ascontiguousarray(w_down.T.astype(f32)))
    shared = {
        "wg1": wg1, "wg2": wg2, "wu": wu, "wv": wv,
        "wgug": wgug, "wguu": wguu, "wd": wd,
        "bu_t": feat_major_vec(bu.astype(f32)),
        "bv_t": feat_major_vec(bv.astype(f32)),
        "bg1_t": feat_major_vec(bg1.astype(f32)),
        "bg2_t": feat_major_vec(bg2.astype(f32)),
        "bg2_t": feat_major_vec(bg2.astype(f32)),
        "cb_t": feat_major_vec(conv_b.astype(f32)),
        "cw_t": np.ascontiguousarray(np.concatenate(
            [feat_major_vec(conv_w[:, 0, k].astype(f32)) for k in range(3)],
            axis=1)),
    }
    in_maps = []
    for i in range(c.n_cores):
        m = dict(shared)
        m["x_t"] = np.ascontiguousarray(x_t[:, i * c.T:(i + 1) * c.T])
        xe = np.zeros((128, 4 * c.NCD), f32)
        for cd in range(c.NCD):
            rows_ = slice(cd * 128, (cd + 1) * 128)
            for h in range(2):
                xe[:, 4 * cd + 2 * h + 0] = x_pad[rows_, i * c.T + h * c.TH]
                xe[:, 4 * cd + 2 * h + 1] = x_pad[rows_, i * c.T + h * c.TH + c.TH + 1]
        m["xe"] = np.ascontiguousarray(xe)
        mask = np.zeros((8, 1), f32)
        mask[:i, 0] = 1.0
        m["mask"] = mask
        in_maps.append(m)
    return in_maps


def kernel(L0_attn_x, L0_attn_Wu, L0_attn_bu, L0_attn_Wv, L0_attn_bv,
           L0_attn_Wg1, L0_attn_bg1, L0_attn_Wg2, L0_attn_bg2,
           L0_attn_conv_w, L0_attn_conv_b, L0_ffn_w_gate_up, L0_ffn_w_down):
    cfg = Cfg()
    in_maps = make_in_maps(cfg, np.asarray(L0_attn_x), np.asarray(L0_attn_Wu),
                           np.asarray(L0_attn_bu), np.asarray(L0_attn_Wv),
                           np.asarray(L0_attn_bv), np.asarray(L0_attn_Wg1),
                           np.asarray(L0_attn_bg1), np.asarray(L0_attn_Wg2),
                           np.asarray(L0_attn_bg2), np.asarray(L0_attn_conv_w),
                           np.asarray(L0_attn_conv_b),
                           np.asarray(L0_ffn_w_gate_up),
                           np.asarray(L0_ffn_w_down))
    nc = build_program(cfg)
    res = run_bass_kernel_spmd(nc, in_maps, core_ids=list(range(cfg.n_cores)))
    outs = [np.asarray(r["out"]).T for r in res.results]
    full = np.concatenate(outs, axis=0)
    return full.reshape(1, cfg.n_cores * cfg.T, cfg.D).astype(np.float32)



def _selftest():
    import concourse.bass_interp as bass_interp

    cfg = Cfg(D=256, DH=128, INTER=512, T=128, n_cores=2, PANEL_NC=1)
    rng = np.random.default_rng(0)
    L = cfg.n_cores * cfg.T
    s = 0.02
    x = rng.standard_normal((1, L, cfg.D), dtype=np.float32)
    Wu = rng.standard_normal((cfg.R, cfg.D), dtype=np.float32) * s
    bu = rng.standard_normal(cfg.R).astype(np.float32) * 0.1
    Wv = rng.standard_normal((cfg.R, cfg.D), dtype=np.float32) * s
    bv = rng.standard_normal(cfg.R).astype(np.float32) * 0.1
    Wg1 = rng.standard_normal((cfg.DH, cfg.D), dtype=np.float32) * s
    bg1 = rng.standard_normal(cfg.DH).astype(np.float32) * 0.1
    Wg2 = rng.standard_normal((cfg.R, cfg.DH), dtype=np.float32) * s
    bg2 = rng.standard_normal(cfg.R).astype(np.float32) * 0.1
    cw = rng.standard_normal((cfg.D, 1, 3), dtype=np.float32) * s
    cb = rng.standard_normal(cfg.D).astype(np.float32) * 0.1
    Wgu = rng.standard_normal((2 * cfg.INTER, cfg.D), dtype=np.float32) * s
    Wd = rng.standard_normal((cfg.D, cfg.INTER), dtype=np.float32) * s

    def ln(a):
        m = a.mean(-1, keepdims=True)
        v = a.var(-1, keepdims=True)
        return (a - m) / np.sqrt(v + 1e-5)

    xs = x[0]
    xp = np.concatenate([np.zeros((1, cfg.D), np.float32), xs,
                         np.zeros((1, cfg.D), np.float32)], axis=0)
    t_out = (xp[:-2] * cw[:, 0, 0] + xp[1:-1] * cw[:, 0, 1]
             + xp[2:] * cw[:, 0, 2] + cb)
    u = xs @ Wu.T + bu
    v = xs @ Wv.T + bv
    h = np.maximum(xs @ Wg1.T + bg1, 0.0)
    g = 1.0 / (1.0 + np.exp(-(h @ Wg2.T + bg2)))
    k = np.cumsum(u * g, axis=0)
    go = k * (v * g)
    uv = go @ Wu.T if cfg.R == cfg.D else None
    uv = np.einsum("tr,dr->td", go, Wu)
    x1 = ln(xs + t_out + uv)
    gu = x1 @ Wgu.T
    gate, up = gu[:, :cfg.INTER], gu[:, cfg.INTER:]
    f = (gate / (1.0 + np.exp(-gate))) * up
    ffn = f @ Wd.T
    ref = ln(x1 + ffn)

    in_maps = make_in_maps(cfg, x, Wu, bu, Wv, bv, Wg1, bg1, Wg2, bg2, cw, cb,
                           Wgu, Wd)
    nc = build_program(cfg)
    n_inst = sum(len(v) for v in nc.m.functions[0].basicblocks[0].instructions
                 ) if False else "?"
    sim = bass_interp.MultiCoreSim(nc, cfg.n_cores)
    for i in range(cfg.n_cores):
        for kk, vv in in_maps[i].items():
            sim.cores[i].tensor(kk)[:] = vv
    sim.simulate()
    outs = [np.asarray(sim.cores[i].mem_tensor("out")).T
            for i in range(cfg.n_cores)]
    got = np.concatenate(outs, axis=0)
    err = np.linalg.norm(got - ref) / np.linalg.norm(ref)
    print("tiny selftest rel_err:", err)
    print("max abs err:", np.abs(got - ref).max())
    assert err < 0.05, err
    print("SELFTEST PASS")


if __name__ == "__main__":
    _selftest()


# version 10
# speedup vs baseline: 1.0095x; 1.0095x over previous
import numpy as np
import ml_dtypes
import concourse.bass as bass
import concourse.mybir as mybir
from concourse.bass_utils import run_bass_kernel_spmd

F32 = mybir.dt.float32
BF16 = mybir.dt.bfloat16
AF = mybir.ActivationFunctionType
ALU = mybir.AluOpType

ENGINES = ("PE", "ACT", "DVE", "SP", "POOL")


class Cfg:
    def __init__(self, D=2048, DH=512, INTER=8192, T=1024, n_cores=8, PANEL_NC=16):
        self.D = D
        self.R = D
        self.DH = DH
        self.INTER = INTER
        self.T = T
        self.TH = T // 2
        self.TBW = T // 2
        self.n_cores = n_cores
        self.NCD = D // 128
        self.NCR = self.R // 128
        self.NCH = DH // 128
        self.NCI = INTER // 128
        self.PANEL_NC = PANEL_NC
        self.NPANEL = self.NCI // PANEL_NC
        self.PANEL = PANEL_NC * 128
        assert self.TH <= 512 and self.TBW <= 512
        assert self.NPANEL * PANEL_NC == self.NCI
        self.MAXW = max(D, DH, self.PANEL)



def tile_w(A):
    K, M = A.shape
    KC, MC = K // 128, M // 128
    return np.ascontiguousarray(
        A.reshape(KC, 128, MC, 128).transpose(2, 1, 0, 3).reshape(MC, 128, KC * 128)
    )


def feat_major_vec(b):
    F = b.shape[0]
    return np.ascontiguousarray(b.reshape(F // 128, 128).T)



class Builder:
    def __init__(self, sem_names):
        self.ops = {e: [] for e in ENGINES}
        self.cnt = {n: 0 for n in sem_names}
        self.waited = {e: {} for e in ENGINES}
        self.seq = False
        self.strip_i = 0
        self.cast_val = {}
        self.pe_val_strip = {}

    def emit(self, eng, fn, deps=(), inc=None, inc_amt=1):
        if self.seq:
            deps = list(deps) + [(s, v) for s, v in self.cnt.items() if v > 0]
        real = []
        w = self.waited[eng]
        for s, v in deps:
            if v > 0 and w.get(s, 0) < v:
                real.append((s, v))
                w[s] = v
        self.ops[eng].append((tuple(real), fn, inc, inc_amt))
        if inc is not None:
            self.cnt[inc] += inc_amt
            return self.cnt[inc]
        return None

    def wait(self, eng, deps):
        self.emit(eng, None, deps=deps)


class Bank:
    __slots__ = ("t", "last", "mmv")

    def __init__(self, t):
        self.t = t
        self.last = None



def build_program(cfg, stage=None, dbg_nog=False, dbg_noG_read=False, seq=False):
    c = cfg
    nc = bass.Bass("TRN2", target_bir_lowering=False)

    dt = {}
    def din(name, shape):
        dt[name] = nc.dram_tensor(name, list(shape), F32, kind="ExternalInput")
        return dt[name]

    din("x_t", (c.D, c.T))
    din("xe", (128, 4 * c.NCD))
    din("wg1", (c.NCH, 128, c.D))
    din("wg2", (c.NCR, 128, c.DH))
    din("wu", (c.NCR, 128, c.D))
    din("wv", (c.NCR, 128, c.D))
    din("wgug", (c.NCI, 128, c.D))
    din("wguu", (c.NCI, 128, c.D))
    din("wd", (c.NCD, 128, c.INTER))
    din("bu_t", (128, c.NCR))
    din("bv_t", (128, c.NCR))
    din("bg1_t", (128, c.NCH))
    din("bg2_t", (128, c.NCR))
    din("cb_t", (128, c.NCD))
    din("cw_t", (128, 3 * c.NCD))
    din("mask", (8, 1))
    out_d = nc.dram_tensor("out", [c.D, c.T], F32, kind="ExternalOutput")
    cc_in = nc.dram_tensor("cc_in", [128, c.NCR], F32)
    cc_out = nc.dram_tensor("cc_out", [c.n_cores * 128, c.NCR], F32)

    sb = {}
    def s(name, shape, dtype):
        sb[name] = nc.alloc_sbuf_tensor(name, list(shape), dtype)
        return sb[name]

    A = s("A", (128, c.NCD * c.T), F32)
    X = s("X", (128, c.NCD * c.TH), BF16)
    G = s("G", (128, c.NCR * c.TH), BF16)
    U = s("U", (128, c.NCR * c.T), BF16)
    V = s("V", (128, c.NCR * c.T), BF16)
    Hh = s("Hh", (128, c.NCH * c.TH), BF16)
    w32 = [s(f"w32_{i}", (128, c.MAXW), F32) for i in range(2)]
    w16 = [s(f"w16_{i}", (128, c.MAXW), BF16) for i in range(2)]
    scr_n = max(2 * (c.TH + 2) + 2 * c.TH, 2 * c.TBW, c.NCR * 128)
    SCR = s("SCR", (128, scr_n), F32)
    zeros = s("zeros", (128, c.TH), BF16)
    rows = s("rows", (1, 3 * c.TBW), F32)
    S_sb = s("S_sb", (128, c.NCR), F32)
    C_sb = s("C_sb", (128, c.NCR), F32)
    bu_s = s("bu_s", (128, c.NCR), F32)
    bv_s = s("bv_s", (128, c.NCR), F32)
    bg1_s = s("bg1_s", (128, c.NCH), F32)
    bg2_s = s("bg2_s", (128, c.NCR), F32)
    cb_s = s("cb_s", (128, c.NCD), F32)
    cw_s = s("cw_s", (128, 3 * c.NCD), F32)
    xe_s = s("xe_s", (128, 4 * c.NCD), F32)
    mask_s = s("mask_s", (8, 1), F32)
    ones_b = s("ones_b", (128, 1), BF16)
    ones_f = s("ones_f", (1, 128), F32)
    eps_s = s("eps_s", (1, 1), F32)

    TH2 = c.TH + 2
    tmp0 = SCR.ap()[:, 2 * TH2: 2 * TH2 + c.TH]
    tmp1 = SCR.ap()[:, 2 * TH2 + c.TH: 2 * TH2 + 2 * c.TH]
    S_all = SCR.ap()[0:c.n_cores, 0:c.NCR * 128]
    sg = [SCR.ap()[:, i * c.TBW:(i + 1) * c.TBW] for i in range(2)]

    pm = [Bank(nc.alloc_psum_tensor(f"pm{i}", [128, 512], F32)) for i in range(4)]
    pbc = [Bank(nc.alloc_psum_tensor(f"pbc{i}", [128, 512], F32)) for i in range(2)]
    pst = [Bank(nc.alloc_psum_tensor(f"pst{i}", [128, 512], F32)) for i in range(2)]
    bank_rot = [0]

    def next_bank():
        b = pm[bank_rot[0] % 4]
        bank_rot[0] += 1
        return b

    sem_names = ["pe", "act", "dve", "pool", "cc",
                 "wdma0", "wdma1", "xdma0", "xdma1", "pdma", "odma", "cdma"]
    semh = {n: nc.alloc_semaphore(n) for n in sem_names}
    B = Builder(sem_names)
    B.seq = seq
    B.scr_last = {0: None, 1: None}

    for name, tgt in (("bu_t", bu_s), ("bv_t", bv_s), ("bg1_t", bg1_s),
                      ("bg2_t", bg2_s), ("cb_t", cb_s), ("cw_t", cw_s),
                      ("xe", xe_s), ("mask", mask_s)):
        B.emit("SP", lambda e, i=dt[name].ap(), o=tgt.ap(): e.dma_start(out=o, in_=i),
               inc="pdma", inc_amt=16)
    bias_val = B.cnt["pdma"]
    B.emit("DVE", lambda e: e.memset(zeros.ap(), 0.0), inc="dve")
    B.emit("DVE", lambda e: e.memset(ones_b.ap(), 1.0), inc="dve")
    B.emit("DVE", lambda e: e.memset(ones_f.ap(), 1.0), inc="dve")
    B.emit("DVE", lambda e: e.memset(eps_s.ap(), 1e-5), inc="dve")
    const_val = B.cnt["dve"]

    def load_strip(dram_ap, width, extra_cast_deps=()):
        si = B.strip_i
        B.strip_i += 1
        slot = si % 2
        dma_sem = "wdma0" if si % 2 == 0 else "wdma1"
        deps = []
        if si - 2 in B.cast_val:
            deps.append(("act", B.cast_val[si - 2]))
        B.emit("SP", lambda e, i=dram_ap, o=w32[slot].ap()[:, 0:width]:
               e.dma_start(out=o, in_=i), deps=deps, inc=dma_sem, inc_amt=16)
        dv = B.cnt[dma_sem]
        cdeps = [(dma_sem, dv)] + list(extra_cast_deps)
        if si - 2 in B.pe_val_strip:
            cdeps.append(("pe", B.pe_val_strip[si - 2]))
        cv = B.emit("ACT", lambda e, o=w16[slot].ap()[:, 0:width],
                    i=w32[slot].ap()[:, 0:width]: e.copy(o, i),
                    deps=cdeps, inc="act")
        B.cast_val[si] = cv
        return si, slot, cv

    def mm_group(si, slot, KC, rhs_fn, W, extra_deps=(), drain=True,
                 kc_dep=None):
        bank = next_bank()
        deps = [("act", B.cast_val[si])] + list(extra_deps)
        if bank.last:
            deps.append(bank.last)
        for kc in range(KC):
            first, last = kc == 0, kc == KC - 1
            d = list(deps) if first else []
            if kc_dep is not None:
                d.append(kc_dep(kc))
            def mk(e, kc=kc, bank=bank, first=first, last=last):
                return e.matmul(bank.t.ap()[:, 0:W],
                                w16[slot].ap()[:, kc * 128:(kc + 1) * 128],
                                rhs_fn(kc),
                                start=first, stop=last)
            B.emit("PE", mk, deps=d)
        if not drain:
            return bank, None
        mmv = B.emit("PE", lambda e: e.drain(), inc="pe")
        B.pe_val_strip[si] = mmv
        return bank, mmv

    p0_done = {}

    def phase_p0_load(h, queue):
        dsem = "xdma0" if h == 0 else "xdma1"
        for cd in range(c.NCD):
            src = dt["x_t"].ap()[cd * 128:(cd + 1) * 128,
                                 h * c.TH:(h + 1) * c.TH]
            asl = A.ap()[:, cd * c.T + h * c.TH: cd * c.T + (h + 1) * c.TH]
            B.emit(queue, lambda e, i=src, o=asl: e.dma_start(out=o, in_=i),
                   inc=dsem, inc_amt=16)

    def phase_p0_cast(h, pe_guard, extra=()):
        dsem = "xdma0" if h == 0 else "xdma1"
        for cd in range(c.NCD):
            asl = A.ap()[:, cd * c.T + h * c.TH: cd * c.T + (h + 1) * c.TH]
            xsl = X.ap()[:, cd * c.TH:(cd + 1) * c.TH]
            adeps = [(dsem, 16 * c.NCD)] + list(extra)
            if pe_guard is not None:
                adeps.append(pe_guard)
            if cd % 2 == 0:
                B.emit("DVE", lambda e, asl=asl, xsl=xsl: e.tensor_copy(
                    xsl, asl), deps=adeps, inc="dve")
            else:
                B.emit("ACT", lambda e, asl=asl, xsl=xsl: e.copy(xsl, asl),
                       deps=adeps, inc="act")
        p0_done[h] = (("act", B.cnt["act"]), ("dve", B.cnt["dve"]))

    def phase_conv(h):
        cast_dep = p0_done[h][0]
        cast_dep_dve = p0_done[h][1]
        for cd in range(c.NCD):
            w0 = cw_s.ap()[:, 0 * c.NCD + cd: 0 * c.NCD + cd + 1]
            w1 = cw_s.ap()[:, 1 * c.NCD + cd: 1 * c.NCD + cd + 1]
            w2 = cw_s.ap()[:, 2 * c.NCD + cd: 2 * c.NCD + cd + 1]
            cb = cb_s.ap()[:, cd:cd + 1]
            asl = A.ap()[:, cd * c.T + h * c.TH: cd * c.T + (h + 1) * c.TH]
            xsl = X.ap()[:, cd * c.TH:(cd + 1) * c.TH]
            d0 = [cast_dep, cast_dep_dve, ("pdma", bias_val),
                  ("dve", B.cnt["dve"])]
            B.emit("DVE", lambda e, asl=asl, xsl=xsl, w1=w1:
                   e.scalar_tensor_tensor(asl, xsl, w1, asl,
                                          ALU.mult, ALU.add),
                   deps=d0, inc="dve")
            B.emit("DVE", lambda e, asl=asl, cb=cb:
                   e.tensor_scalar_add(asl, asl, cb),
                   deps=[("dve", B.cnt["dve"])], inc="dve")
            B.emit("DVE", lambda e, asl=asl, xsl=xsl, w0=w0:
                   e.scalar_tensor_tensor(asl[:, 1:c.TH], xsl[:, 0:c.TH - 1],
                                          w0, asl[:, 1:c.TH],
                                          ALU.mult, ALU.add),
                   deps=[("dve", B.cnt["dve"])], inc="dve")
            B.emit("DVE", lambda e, asl=asl, xsl=xsl, w2=w2:
                   e.scalar_tensor_tensor(asl[:, 0:c.TH - 1], xsl[:, 1:c.TH],
                                          w2, asl[:, 0:c.TH - 1],
                                          ALU.mult, ALU.add),
                   deps=[("dve", B.cnt["dve"])], inc="dve")
            eL = xe_s.ap()[:, 4 * cd + 2 * h: 4 * cd + 2 * h + 1]
            eR = xe_s.ap()[:, 4 * cd + 2 * h + 1: 4 * cd + 2 * h + 2]
            B.emit("DVE", lambda e, asl=asl, eL=eL, w0=w0:
                   e.scalar_tensor_tensor(asl[:, 0:1], eL, w0, asl[:, 0:1],
                                          ALU.mult, ALU.add),
                   deps=[("dve", B.cnt["dve"])], inc="dve")
            B.emit("DVE", lambda e, asl=asl, eR=eR, w2=w2:
                   e.scalar_tensor_tensor(asl[:, c.TH - 1:c.TH], eR, w2,
                                          asl[:, c.TH - 1:c.TH],
                                          ALU.mult, ALU.add),
                   deps=[("dve", B.cnt["dve"])], inc="dve")

    def x_rhs(kc):
        return X.ap()[:, kc * c.TH:(kc + 1) * c.TH]

    def phase_h(h):
        xdep = [p0_done[h][0], p0_done[h][1]]
        pend = load_strip(dt["wg1"].ap()[0], c.D)
        for oc in range(c.NCH):
            si, slot, _ = pend
            bank, mmv = mm_group(si, slot, c.NCD, x_rhs, c.TH,
                                 extra_deps=xdep if oc == 0 else ())
            if oc + 1 < c.NCH:
                pend = load_strip(dt["wg1"].ap()[oc + 1], c.D)
            osl = Hh.ap()[:, oc * c.TH:(oc + 1) * c.TH]
            ev = B.emit("ACT", lambda e, b=bank, osl=osl, oc=oc: e.activation(
                osl, b.t.ap()[:, 0:c.TH], AF.Relu,
                bias=bg1_s.ap()[:, oc:oc + 1], scale=1.0),
                deps=[("pe", mmv), ("pdma", bias_val)], inc="act")
            bank.last = ("act", ev)

    def phase_gates(h, gdep):
        hdep = [("act", B.cnt["act"])] + gdep
        pend = load_strip(dt["wg2"].ap()[0], c.DH)
        for oc in range(c.NCR):
            si, slot, _ = pend
            bank, mmv = mm_group(si, slot, c.NCH,
                                 lambda kc: Hh.ap()[:, kc * c.TH:(kc + 1) * c.TH],
                                 c.TH, extra_deps=hdep if oc == 0 else ())
            if oc + 1 < c.NCR:
                pend = load_strip(dt["wg2"].ap()[oc + 1], c.DH)
            osl = G.ap()[:, oc * c.TH:(oc + 1) * c.TH]
            deps = [("pe", mmv)] + (gdep if oc == 0 else [])
            ev = B.emit("ACT", lambda e, b=bank, osl=osl, oc=oc: e.activation(
                osl, b.t.ap()[:, 0:c.TH], AF.Sigmoid,
                bias=bg2_s.ap()[:, oc:oc + 1], scale=1.0),
                deps=deps, inc="act")
            bank.last = ("act", ev)

    def phase_uv_proj(h, which):
        wname, bias_t, dst = (("wu", bu_s, U) if which == "u" else ("wv", bv_s, V))
        pend = load_strip(dt[wname].ap()[0], c.D)
        for oc in range(c.NCR):
            si, slot, _ = pend
            bank, mmv = mm_group(si, slot, c.NCD, x_rhs, c.TH)
            if oc + 1 < c.NCR:
                pend = load_strip(dt[wname].ap()[oc + 1], c.D)
            osl = dst.ap()[:, oc * c.T + h * c.TH: oc * c.T + (h + 1) * c.TH]
            gsl = G.ap()[:, oc * c.TH:(oc + 1) * c.TH]
            av = B.emit("ACT", lambda e, b=bank, osl=osl, oc=oc,
                        bias_t=bias_t: e.activation(
                osl, b.t.ap()[:, 0:c.TH], AF.Identity,
                bias=bias_t.ap()[:, oc:oc + 1], scale=1.0),
                deps=[("pe", mmv), ("pdma", bias_val)], inc="act")
            bank.last = ("act", av)
            if not dbg_nog:
                B.emit("DVE", lambda e, osl=osl, gsl=gsl: e.tensor_tensor(
                    osl, osl, gsl, ALU.mult),
                    deps=[("act", av), ("dve", B.cnt["dve"])], inc="dve")
                if which == "u":
                    init = (0.0 if h == 0 else
                            U.ap()[:, oc * c.T + c.TH - 1: oc * c.T + c.TH])
                    B.emit("DVE", lambda e, osl=osl, init=init:
                           e.tensor_tensor_scan(osl, osl, zeros.ap()[:, 0:c.TH],
                                                init, ALU.add, ALU.add),
                           deps=[("dve", B.cnt["dve"])], inc="dve")

    def phase_scan(h):
        udep = ("act", B.cnt["act"])
        for cd in range(c.NCR):
            usl = U.ap()[:, cd * c.T + h * c.TH: cd * c.T + (h + 1) * c.TH]
            if h == 0:
                init = 0.0
            else:
                init = U.ap()[:, cd * c.T + c.TH - 1: cd * c.T + c.TH]
            B.emit("DVE", lambda e, usl=usl, init=init: e.tensor_tensor_scan(
                usl, usl, zeros.ap()[:, 0:c.TH], init, ALU.add, ALU.add),
                deps=[("dve", B.cnt["dve"]), udep], inc="dve")

    def phase_carry_fire():
        sv = B.emit("DVE", lambda e: e.tensor_copy(
            S_sb.ap()[:, 0:c.NCR], U.ap()[:, c.T - 1::c.T]),
            deps=[("dve", B.cnt["dve"])], inc="dve")
        B.emit("POOL", lambda e: e.dma_start(out=cc_in.ap(), in_=S_sb.ap()),
               deps=[("dve", sv)], inc="cdma", inc_amt=16)
        B.wait("POOL", [("cdma", B.cnt["cdma"])])
        B.emit("POOL", lambda e: e.collective_compute(
            "AllGather", ALU.bypass,
            replica_groups=[list(range(c.n_cores))],
            ins=[cc_in.ap()], outs=[cc_out.ap()]), inc="cc")
        B.wait("POOL", [("cc", B.cnt["cc"])])
        sall_dep = [("cc", B.cnt["cc"]),
                    ("dve", B.cnt["dve"]), ("act", B.cnt["act"])]
        B.emit("POOL", lambda e: e.dma_start(
            out=S_all, in_=cc_out.ap().rearrange("(n f) c -> n (f c)", n=c.n_cores)),
            deps=sall_dep, inc="cdma", inc_amt=16)
        return B.cnt["cdma"]

    def phase_carry_finish(pv):
        bank = pbc[0]
        deps = [("cdma", pv), ("pdma", bias_val)]
        if bank.last:
            deps.append(bank.last)
        for cd in range(c.NCR):
            def mk(e, cd=cd):
                return e.matmul(bank.t.ap()[:, cd:cd + 1],
                                S_all[:, cd::c.NCR],
                                mask_s.ap()[0:c.n_cores, 0:1],
                                start=True, stop=True)
            B.emit("PE", mk, deps=(deps if cd == 0 else ()))
        mmv = B.emit("PE", lambda e: e.drain(), inc="pe")
        cv = B.emit("ACT", lambda e: e.copy(
            C_sb.ap()[:, 0:c.NCR], bank.t.ap()[:, 0:c.NCR]),
            deps=[("pe", mmv)], inc="act")
        bank.last = ("act", cv)
        return ("act", cv)

    def phase_go(cdep):
        go_val = {}
        for cd in range(c.NCR):
            for h in range(2):
                ksl = U.ap()[:, cd * c.T + h * c.TH: cd * c.T + (h + 1) * c.TH]
                vsl = V.ap()[:, cd * c.T + h * c.TH: cd * c.T + (h + 1) * c.TH]
                v = B.emit("DVE", lambda e, ksl=ksl, vsl=vsl, cd=cd:
                           e.scalar_tensor_tensor(vsl, ksl,
                                                  C_sb.ap()[:, cd:cd + 1],
                                                  vsl, ALU.add, ALU.mult),
                           deps=[("dve", B.cnt["dve"]), cdep], inc="dve")
                go_val[(cd, h)] = v
        return go_val

    def phase_uvmm(go_val):
        pend = load_strip(dt["wu"].ap()[0], c.R)
        gi = [0]
        for oc in range(c.NCD):
            si, slot, _ = pend
            grps = []
            for tb in range(2):
                bank, mmv = mm_group(
                    si, slot, c.NCR,
                    lambda kc, tb=tb: V.ap()[:, kc * c.T + tb * c.TBW:
                                             kc * c.T + (tb + 1) * c.TBW],
                    c.TBW,
                    drain=(tb == 1),
                    kc_dep=((lambda kc, tb=tb: ("dve", go_val[(kc, tb)]))
                            if oc == 0 else None))
                grps.append((tb, bank, mmv))
            mmv = grps[1][2]
            grps[0] = (0, grps[0][1], mmv)
            B.pe_val_strip[si] = B.cnt["pe"]
            if oc + 1 < c.NCD:
                pend = load_strip(dt["wu"].ap()[oc + 1], c.R)
            for tb, bank, mmv in grps:
                sl = gi[0] % 2
                gi[0] += 1
                sdeps = [("pe", mmv)]
                if B.scr_last[sl]:
                    sdeps.append(B.scr_last[sl])
                av = B.emit("ACT", lambda e, b=bank, sl=sl: e.copy(
                    sg[sl], b.t.ap()[:, 0:c.TBW]), deps=sdeps, inc="act")
                bank.last = ("act", av)
                asl = A.ap()[:, oc * c.T + tb * c.TBW: oc * c.T + (tb + 1) * c.TBW]
                dv = B.emit("DVE", lambda e, asl=asl, sl=sl: e.tensor_tensor(
                    asl, asl, sg[sl], ALU.add),
                    deps=[("act", av), ("dve", B.cnt["dve"])], inc="dve")
                B.scr_last[sl] = ("dve", dv)

    def phase_ln(do_x2b=False):
        a3 = A.ap().rearrange("p (c t) -> p c t", t=c.T)
        x3 = X.ap().rearrange("p (c t) -> p c t", t=c.TH)
        g3 = G.ap().rearrange("p (c t) -> p c t", t=c.TH)
        u3 = U.ap().rearrange("p (c t) -> p c t", t=c.T)
        ln_vals = []
        apply_cd = {}
        x2b_vals = []
        stage_val = {}
        stats_mmv = {}
        stats_banks = {0: pst, 1: pbc}
        for tb in range(2):
            adep = [("dve", B.cnt["dve"]), ("pe", B.cnt["pe"])]
            if tb == 1:
                adep.append(("pe", stats_mmv[(0, 1)]))
            cv = B.emit("DVE", lambda e, tb=tb: e.tensor_copy(
                x3, a3[:, :, tb * c.TBW:(tb + 1) * c.TBW]),
                deps=adep, inc="dve")
            sv = B.emit("DVE", lambda e: e.tensor_tensor(
                g3, x3, x3, ALU.mult), deps=[("dve", cv)], inc="dve")
            stage_val[tb] = (cv, sv)
            for bi, src_t in ((0, X), (1, G)):
                bank = stats_banks[tb][bi]
                deps = [("dve", cv if bi == 0 else sv)]
                if bank.last:
                    deps.append(bank.last)
                for cd in range(c.NCD):
                    def mk(e, cd=cd, bank=bank, src_t=src_t,
                           first=(cd == 0), last=(cd == c.NCD - 1)):
                        return e.matmul(bank.t.ap()[0:1, 0:c.TBW],
                                        ones_b.ap()[:, 0:1],
                                        src_t.ap()[:, cd * c.TH:(cd + 1) * c.TH],
                                        start=first, stop=last)
                    B.emit("PE", mk, deps=(deps if cd == 0 else ()))
                stats_mmv[(tb, bi)] = B.emit("PE", lambda e: e.drain(),
                                             inc="pe")
        r_negmu = rows.ap()[0:1, 0:c.TBW]
        r_var = rows.ap()[0:1, c.TBW:2 * c.TBW]
        r_tmp = rows.ap()[0:1, 2 * c.TBW:3 * c.TBW]
        prev_bm = {}
        for tb in range(2):
            sb0, sb1 = stats_banks[tb]
            bc_dst = (sg if tb == 0 else [tmp0, tmp1])
            v1deps = [("pe", stats_mmv[(tb, 0)])]
            v1bdeps = [("pe", stats_mmv[(tb, 1)]), ("dve", B.cnt["dve"])]
            if tb == 1:
                v1deps.append(("pe", prev_bm[0]))
                v1bdeps.append(("pe", prev_bm[1]))
            v1 = B.emit("ACT", lambda e, sb0=sb0: e.activation(
                r_negmu, sb0.t.ap()[0:1, 0:c.TBW], AF.Copy,
                bias=0.0, scale=-1.0 / c.D), deps=v1deps, inc="act")
            sb0.last = ("act", v1)
            v1b = B.emit("ACT", lambda e, sb1=sb1: e.activation(
                r_var, sb1.t.ap()[0:1, 0:c.TBW], AF.Copy,
                bias=0.0, scale=1.0 / c.D), deps=v1bdeps, inc="act")
            sb1.last = ("act", v1b)
            v2 = B.emit("DVE", lambda e: e.tensor_tensor(
                r_tmp, r_negmu, r_negmu, ALU.mult),
                deps=[("act", v1), ("dve", B.cnt["dve"])], inc="dve")
            v3 = B.emit("DVE", lambda e: e.tensor_tensor(
                r_var, r_var, r_tmp, ALU.subtract),
                deps=[("act", v1b), ("dve", v2)], inc="dve")
            v4 = B.emit("ACT", lambda e: e.activation(
                r_tmp, r_var, AF.Sqrt, bias=eps_s.ap()[0:1, 0:1], scale=1.0),
                deps=[("dve", v3), ("dve", const_val)], inc="act")
            v5 = B.emit("DVE", lambda e: e.reciprocal(r_var, r_tmp),
                        deps=[("act", v4), ("dve", v3)], inc="dve")
            bdeps0 = [("act", v1)] + ([sb0.last] if sb0.last else [])
            B.emit("PE", lambda e, sb0=sb0: e.matmul(
                sb0.t.ap()[:, 0:c.TBW], ones_f.ap()[0:1, :], r_negmu,
                start=True, stop=True), deps=bdeps0)
            bm0 = B.emit("PE", lambda e: e.drain(), inc="pe")
            bdeps1 = [("dve", v5)]
            B.emit("PE", lambda e, sb1=sb1: e.matmul(
                sb1.t.ap()[:, 0:c.TBW], ones_f.ap()[0:1, :], r_var,
                start=True, stop=True), deps=bdeps1)
            bm1 = B.emit("PE", lambda e: e.drain(), inc="pe")
            prev_bm = {0: bm0, 1: bm1}
            e0deps = [("pe", bm0), ("dve", B.cnt["dve"])]
            e1deps = [("pe", bm1)]
            if tb == 0:
                for k in range(2):
                    if B.scr_last[k]:
                        (e0deps if k == 0 else e1deps).append(B.scr_last[k])
            bc0v = B.emit("ACT", lambda e, sb0=sb0, d=bc_dst[0]: e.copy(
                d, sb0.t.ap()[:, 0:c.TBW]), deps=e0deps, inc="act")
            bc1v = B.emit("ACT", lambda e, sb1=sb1, d=bc_dst[1]: e.copy(
                d, sb1.t.ap()[:, 0:c.TBW]), deps=e1deps, inc="act")
            sb0.last = ("act", bc0v)
            sb1.last = ("act", bc1v)
            ev = None
            for cd in range(c.NCD):
                asl = A.ap()[:, cd * c.T + tb * c.TBW: cd * c.T + (tb + 1) * c.TBW]
                B.emit("DVE", lambda e, asl=asl, d=bc_dst[0]: e.tensor_tensor(
                    asl, asl, d, ALU.add),
                    deps=[("act", bc0v)], inc="dve")
                ev = B.emit("DVE", lambda e, asl=asl, d=bc_dst[1]: e.tensor_tensor(
                    asl, asl, d, ALU.mult),
                    deps=[("act", bc1v), ("dve", B.cnt["dve"])], inc="dve")
                if tb == 1:
                    apply_cd[cd] = ev
            if tb == 0:
                B.scr_last = {0: ("dve", ev), 1: ("dve", ev)}
            B.scr_tmp_last = ("dve", ev)
            ln_vals.append(ev)
            if do_x2b:
                v = B.emit("ACT", lambda e, tb=tb: e.copy(
                    u3[:, :, tb * c.TBW:(tb + 1) * c.TBW],
                    a3[:, :, tb * c.TBW:(tb + 1) * c.TBW]),
                    deps=[("dve", ev)], inc="act")
                x2b_vals.append([("act", v)])
        return ln_vals, apply_cd, x2b_vals

    def phase_ffn(x2b_done):
        descs = []
        for p in range(c.NPANEL):
            for j in range(c.PANEL_NC):
                ocg = p * c.PANEL_NC + j
                descs.append(("g", p, j, dt["wgug"].ap()[ocg], c.D))
                descs.append(("u", p, j, dt["wguu"].ap()[ocg], c.D))
            for oc in range(c.NCD):
                descs.append(("d", p, oc,
                              dt["wd"].ap()[oc][:, p * c.PANEL:(p + 1) * c.PANEL],
                              c.PANEL))
        pend = load_strip(descs[0][3], descs[0][4])
        f_done = {}
        for i, (kind, p, idx, ap_, w) in enumerate(descs):
            si, slot, _ = pend
            grps = []
            for tb in range(2):
                if kind == "d":
                    rhs = lambda kc, tb=tb: V.ap()[:, kc * c.T + tb * c.TBW:
                                                   kc * c.T + (tb + 1) * c.TBW]
                    extra = [f_done[p]] if (idx == 0 and tb == 0) else ()
                    bank, mmv = mm_group(si, slot, c.PANEL_NC, rhs, c.TBW,
                                         extra_deps=extra, drain=(tb == 1))
                else:
                    rhs = lambda kc, tb=tb: U.ap()[:, kc * c.T + tb * c.TBW:
                                                   kc * c.T + (tb + 1) * c.TBW]
                    extra = x2b_done[tb] if i == 0 else ()
                    bank, mmv = mm_group(si, slot, c.NCD, rhs, c.TBW,
                                         extra_deps=extra, drain=(tb == 1))
                grps.append((tb, bank, mmv))
            mmv = grps[1][2]
            grps[0] = (0, grps[0][1], mmv)
            B.pe_val_strip[si] = B.cnt["pe"]
            if i + 1 < len(descs):
                pend = load_strip(descs[i + 1][3], descs[i + 1][4])
            for tb, bank, mmv in grps:
                if kind == "g":
                    sdeps = [("pe", mmv)]
                    if B.scr_last[tb]:
                        sdeps.append(B.scr_last[tb])
                    sv = B.emit("ACT", lambda e, b=bank, tb=tb: e.activation(
                        sg[tb], b.t.ap()[:, 0:c.TBW], AF.Silu),
                        deps=sdeps, inc="act")
                    bank.last = ("act", sv)
                elif kind == "u":
                    fsl = V.ap()[:, idx * c.T + tb * c.TBW:
                                 idx * c.T + (tb + 1) * c.TBW]
                    av = B.emit("ACT", lambda e, b=bank, fsl=fsl: e.copy(
                        fsl, b.t.ap()[:, 0:c.TBW]), deps=[("pe", mmv)], inc="act")
                    bank.last = ("act", av)
                    fv = B.emit("DVE", lambda e, fsl=fsl, tb=tb: e.tensor_tensor(
                        fsl, fsl, sg[tb], ALU.mult),
                        deps=[("act", av), ("dve", B.cnt["dve"])], inc="dve")
                    B.scr_last[tb] = ("dve", fv)
                    if idx == c.PANEL_NC - 1 and tb == 1:
                        f_done[p] = ("dve", fv)
                else:
                    sdeps = [("pe", mmv)]
                    if B.scr_last[tb]:
                        sdeps.append(B.scr_last[tb])
                    av = B.emit("ACT", lambda e, b=bank, tb=tb: e.copy(
                        sg[tb], b.t.ap()[:, 0:c.TBW]), deps=sdeps, inc="act")
                    bank.last = ("act", av)
                    asl = A.ap()[:, idx * c.T + tb * c.TBW:
                                 idx * c.T + (tb + 1) * c.TBW]
                    dv = B.emit("DVE", lambda e, asl=asl, tb=tb: e.tensor_tensor(
                        asl, asl, sg[tb], ALU.add),
                        deps=[("act", av), ("dve", B.cnt["dve"])], inc="dve")
                    B.scr_last[tb] = ("dve", dv)

    def phase_out(apply_cd=None):
        dep = [("dve", B.cnt["dve"])]
        for cd in range(c.NCD):
            d = ([("dve", apply_cd[cd])] if apply_cd is not None
                 else (dep if cd == 0 else ()))
            B.emit("SP", lambda e, cd=cd: e.dma_start(
                out=out_d.ap()[cd * 128:(cd + 1) * 128, :],
                in_=A.ap()[:, cd * c.T:(cd + 1) * c.T]),
                deps=d, inc="odma", inc_amt=16)
        B.wait("SP", [("odma", B.cnt["odma"])])

    def dump(src_t):
        for cd in range(c.NCR):
            B.emit("DVE", lambda e, cd=cd: e.tensor_copy(
                A.ap()[:, cd * c.T:(cd + 1) * c.T],
                src_t.ap()[:, cd * c.T:(cd + 1) * c.T]),
                deps=[("dve", B.cnt["dve"]), ("act", B.cnt["act"])], inc="dve")

    def sched():
        phase_p0_load(0, "ACT")
        phase_p0_load(1, "POOL")
        phase_p0_cast(0, None)
        phase_conv(0)
        conv0_dve = ("dve", B.cnt["dve"])
        if stage == "acc0h0":
            return
        phase_h(0)
        phase_gates(0, [("dve", const_val)])
        phase_uv_proj(0, "u")
        if dbg_nog:
            phase_scan(0)
        phase_uv_proj(0, "v")
        v_h0_pe = ("pe", B.cnt["pe"])
        v_h0_dve = ("dve", B.cnt["dve"])
        phase_p0_cast(1, v_h0_pe, extra=[conv0_dve])
        phase_conv(1)
        if stage == "acc0":
            return
        phase_h(1)
        phase_gates(1, [v_h0_dve])
        phase_uv_proj(1, "u")
        if stage == "ug":
            dump(U)
            return
        if dbg_nog:
            phase_scan(1)
        if stage == "k":
            dump(U)
            return
        pv = phase_carry_fire()
        phase_uv_proj(1, "v")
        if stage == "vg":
            dump(V)
            return
        cdep = phase_carry_finish(pv)
        go_val = phase_go(cdep)
        if stage == "go":
            dump(V)
            return
        phase_uvmm(go_val)
        if stage == "res1":
            return
        _, _, x2b_done = phase_ln(do_x2b=True)
        if stage == "x2":
            return
        phase_ffn(x2b_done)
        if stage == "res2":
            return None
        _, apply_cd, _ = phase_ln()
        return apply_cd

    apply_cd = sched()
    phase_out(apply_cd)

    with nc.Block() as blk:
        def runner(ops):
            def run(eng):
                for deps, fn, inc, amt in ops:
                    for sn, v in deps:
                        eng.wait_ge(semh[sn], v)
                    if fn is not None:
                        inst = fn(eng)
                        if inc is not None:
                            inst.then_inc(semh[inc], amt)
            return run
        blk.tensor(runner(B.ops["PE"]))
        blk.scalar(runner(B.ops["ACT"]))
        blk.vector(runner(B.ops["DVE"]))
        blk.sync(runner(B.ops["SP"]))
        blk.gpsimd(runner(B.ops["POOL"]))

    return nc



def make_in_maps(cfg, x, Wu, bu, Wv, bv, Wg1, bg1, Wg2, bg2, conv_w, conv_b,
                 w_gate_up, w_down):
    c = cfg
    f32 = np.float32
    x_t = np.ascontiguousarray(x.reshape(-1, c.D).T.astype(f32))
    L = x_t.shape[1]
    x_pad = np.concatenate([np.zeros((c.D, 1), f32), x_t,
                            np.zeros((c.D, 1), f32)], axis=1)
    wg1 = tile_w(np.ascontiguousarray(Wg1.T.astype(f32)))
    wg2 = tile_w(np.ascontiguousarray(Wg2.T.astype(f32)))
    wu = tile_w(np.ascontiguousarray(Wu.T.astype(f32)))
    wv = tile_w(np.ascontiguousarray(Wv.T.astype(f32)))
    WguT = np.ascontiguousarray(w_gate_up.T.astype(f32))
    wgug = tile_w(WguT[:, :c.INTER])
    wguu = tile_w(WguT[:, c.INTER:])
    wd = tile_w(np.ascontiguousarray(w_down.T.astype(f32)))
    shared = {
        "wg1": wg1, "wg2": wg2, "wu": wu, "wv": wv,
        "wgug": wgug, "wguu": wguu, "wd": wd,
        "bu_t": feat_major_vec(bu.astype(f32)),
        "bv_t": feat_major_vec(bv.astype(f32)),
        "bg1_t": feat_major_vec(bg1.astype(f32)),
        "bg2_t": feat_major_vec(bg2.astype(f32)),
        "bg2_t": feat_major_vec(bg2.astype(f32)),
        "cb_t": feat_major_vec(conv_b.astype(f32)),
        "cw_t": np.ascontiguousarray(np.concatenate(
            [feat_major_vec(conv_w[:, 0, k].astype(f32)) for k in range(3)],
            axis=1)),
    }
    in_maps = []
    for i in range(c.n_cores):
        m = dict(shared)
        m["x_t"] = np.ascontiguousarray(x_t[:, i * c.T:(i + 1) * c.T])
        xe = np.zeros((128, 4 * c.NCD), f32)
        for cd in range(c.NCD):
            rows_ = slice(cd * 128, (cd + 1) * 128)
            for h in range(2):
                xe[:, 4 * cd + 2 * h + 0] = x_pad[rows_, i * c.T + h * c.TH]
                xe[:, 4 * cd + 2 * h + 1] = x_pad[rows_, i * c.T + h * c.TH + c.TH + 1]
        m["xe"] = np.ascontiguousarray(xe)
        mask = np.zeros((8, 1), f32)
        mask[:i, 0] = 1.0
        m["mask"] = mask
        in_maps.append(m)
    return in_maps


def kernel(L0_attn_x, L0_attn_Wu, L0_attn_bu, L0_attn_Wv, L0_attn_bv,
           L0_attn_Wg1, L0_attn_bg1, L0_attn_Wg2, L0_attn_bg2,
           L0_attn_conv_w, L0_attn_conv_b, L0_ffn_w_gate_up, L0_ffn_w_down):
    cfg = Cfg()
    in_maps = make_in_maps(cfg, np.asarray(L0_attn_x), np.asarray(L0_attn_Wu),
                           np.asarray(L0_attn_bu), np.asarray(L0_attn_Wv),
                           np.asarray(L0_attn_bv), np.asarray(L0_attn_Wg1),
                           np.asarray(L0_attn_bg1), np.asarray(L0_attn_Wg2),
                           np.asarray(L0_attn_bg2), np.asarray(L0_attn_conv_w),
                           np.asarray(L0_attn_conv_b),
                           np.asarray(L0_ffn_w_gate_up),
                           np.asarray(L0_ffn_w_down))
    nc = build_program(cfg)
    res = run_bass_kernel_spmd(nc, in_maps, core_ids=list(range(cfg.n_cores)))
    outs = [np.asarray(r["out"]).T for r in res.results]
    full = np.concatenate(outs, axis=0)
    return full.reshape(1, cfg.n_cores * cfg.T, cfg.D).astype(np.float32)



def _selftest():
    import concourse.bass_interp as bass_interp

    cfg = Cfg(D=256, DH=128, INTER=512, T=128, n_cores=2, PANEL_NC=1)
    rng = np.random.default_rng(0)
    L = cfg.n_cores * cfg.T
    s = 0.02
    x = rng.standard_normal((1, L, cfg.D), dtype=np.float32)
    Wu = rng.standard_normal((cfg.R, cfg.D), dtype=np.float32) * s
    bu = rng.standard_normal(cfg.R).astype(np.float32) * 0.1
    Wv = rng.standard_normal((cfg.R, cfg.D), dtype=np.float32) * s
    bv = rng.standard_normal(cfg.R).astype(np.float32) * 0.1
    Wg1 = rng.standard_normal((cfg.DH, cfg.D), dtype=np.float32) * s
    bg1 = rng.standard_normal(cfg.DH).astype(np.float32) * 0.1
    Wg2 = rng.standard_normal((cfg.R, cfg.DH), dtype=np.float32) * s
    bg2 = rng.standard_normal(cfg.R).astype(np.float32) * 0.1
    cw = rng.standard_normal((cfg.D, 1, 3), dtype=np.float32) * s
    cb = rng.standard_normal(cfg.D).astype(np.float32) * 0.1
    Wgu = rng.standard_normal((2 * cfg.INTER, cfg.D), dtype=np.float32) * s
    Wd = rng.standard_normal((cfg.D, cfg.INTER), dtype=np.float32) * s

    def ln(a):
        m = a.mean(-1, keepdims=True)
        v = a.var(-1, keepdims=True)
        return (a - m) / np.sqrt(v + 1e-5)

    xs = x[0]
    xp = np.concatenate([np.zeros((1, cfg.D), np.float32), xs,
                         np.zeros((1, cfg.D), np.float32)], axis=0)
    t_out = (xp[:-2] * cw[:, 0, 0] + xp[1:-1] * cw[:, 0, 1]
             + xp[2:] * cw[:, 0, 2] + cb)
    u = xs @ Wu.T + bu
    v = xs @ Wv.T + bv
    h = np.maximum(xs @ Wg1.T + bg1, 0.0)
    g = 1.0 / (1.0 + np.exp(-(h @ Wg2.T + bg2)))
    k = np.cumsum(u * g, axis=0)
    go = k * (v * g)
    uv = go @ Wu.T if cfg.R == cfg.D else None
    uv = np.einsum("tr,dr->td", go, Wu)
    x1 = ln(xs + t_out + uv)
    gu = x1 @ Wgu.T
    gate, up = gu[:, :cfg.INTER], gu[:, cfg.INTER:]
    f = (gate / (1.0 + np.exp(-gate))) * up
    ffn = f @ Wd.T
    ref = ln(x1 + ffn)

    in_maps = make_in_maps(cfg, x, Wu, bu, Wv, bv, Wg1, bg1, Wg2, bg2, cw, cb,
                           Wgu, Wd)
    nc = build_program(cfg)
    n_inst = sum(len(v) for v in nc.m.functions[0].basicblocks[0].instructions
                 ) if False else "?"
    sim = bass_interp.MultiCoreSim(nc, cfg.n_cores)
    for i in range(cfg.n_cores):
        for kk, vv in in_maps[i].items():
            sim.cores[i].tensor(kk)[:] = vv
    sim.simulate()
    outs = [np.asarray(sim.cores[i].mem_tensor("out")).T
            for i in range(cfg.n_cores)]
    got = np.concatenate(outs, axis=0)
    err = np.linalg.norm(got - ref) / np.linalg.norm(ref)
    print("tiny selftest rel_err:", err)
    print("max abs err:", np.abs(got - ref).max())
    assert err < 0.05, err
    print("SELFTEST PASS")


if __name__ == "__main__":
    _selftest()
